# version 01
# baseline (speedup 1.0000x reference)
import numpy as np
import concourse.bass as bass
import concourse.mybir as mybir
from concourse.bass_utils import run_bass_kernel_spmd

F32 = mybir.dt.float32
BF16 = mybir.dt.bfloat16
I32 = mybir.dt.int32
AF = mybir.ActivationFunctionType
ALU = mybir.AluOpType

NCORES = 8
T = 1024
D = 4096
KC = 32
QH = 8

ENGINES = ("sync", "scalar", "vector", "tensor", "gpsimd")


class Plan:

    def __init__(self, nc):
        self.nc = nc
        self.ops = {e: [] for e in ENGINES}
        self.handles = {}
        self.counts = {}
        self._ctx = []

    def sem(self, name):
        cm = self.nc.semaphore(name)
        self.handles[name] = cm.__enter__()
        self._ctx.append(cm)
        self.counts[name] = 0

    def _flat(self, w, out):
        if w is None:
            return
        if isinstance(w, list):
            for x in w:
                self._flat(x, out)
            return
        s, v = w
        assert v <= self.counts[s], (s, v, self.counts[s])
        out.append((s, v))

    def op(self, eng, fn, *args, wait=(), inc=None, amt=1, **kw):
        w = []
        self._flat(list(wait), w)
        self.ops[eng].append((fn, args, kw, w, inc, amt))
        if inc is not None:
            self.counts[inc] += amt
            return (inc, self.counts[inc])
        return None

    def emit(self):
        nc = self.nc
        plan = self

        def mk(name):
            def emit_fn(eng):
                floor = {}
                for (fn, args, kw, waits, inc, amt) in plan.ops[name]:
                    for (s, v) in waits:
                        if floor.get(s, 0) >= v:
                            continue
                        eng.wait_ge(plan.handles[s], v)
                        floor[s] = v
                    if fn is None:
                        continue
                    inst = getattr(eng, fn)(*args, **kw)
                    if inc is not None:
                        inst.then_inc(plan.handles[inc], amt)
            return emit_fn

        with nc.Block() as block:
            for name in ENGINES:
                getattr(block, name)(mk(name))


def build_nc():
    nc = bass.Bass("TRN2", target_bir_lowering=False)

    xT = nc.dram_tensor("xT", [D, T], F32, kind="ExternalInput").ap()
    wqT = nc.dram_tensor("wqT", [D, 1024], F32, kind="ExternalInput").ap()
    bqm = nc.dram_tensor("bqm", [128, QH], F32, kind="ExternalInput").ap()
    mz = nc.dram_tensor("mz", [QH, 128, 129], F32, kind="ExternalInput").ap()
    b1T = nc.dram_tensor("b1T", [D, D], I32, kind="ExternalInput").ap()
    b3T = nc.dram_tensor("b3T", [D, D], I32, kind="ExternalInput").ap()
    woT = nc.dram_tensor("woT", [D, D], F32, kind="ExternalInput").ap()
    s0m = nc.dram_tensor("s0m", [128, KC], F32, kind="ExternalInput").ap()
    s2m = nc.dram_tensor("s2m", [128, KC], F32, kind="ExternalInput").ap()
    s4m = nc.dram_tensor("s4m", [128, KC], F32, kind="ExternalInput").ap()
    bim = nc.dram_tensor("bim", [128, KC], F32, kind="ExternalInput").ap()
    outT = nc.dram_tensor("outT", [D, T], F32, kind="ExternalOutput").ap()

    TOTAL = 200 * 1024
    arena = nc.sbuf_tensor("arena", [128, TOTAL], mybir.dt.uint8).__enter__()
    base = nc.lookup_mloc(arena).addr

    def at(name, shape, dtype, off):
        h = nc.virtual_sbuf_tensor(name, shape, dtype)
        nc._tensor(h.name, list(h.shape), h.dtype, type="SB")
        mloc = nc.lookup_mloc(h)
        mloc.allocated = True
        mloc.addr = base + off
        return h

    XW, HW, RET, SB1, WQ3, SMALL = 0, 65536, 131072, 147456, 163840, 180224
    xw = [at(f"xw{k}", [128, T], BF16, XW + 2048 * k) for k in range(KC)]
    hw = [at(f"hw{k}", [128, T], BF16, HW + 2048 * k) for k in range(KC)]
    ret = [at(f"ret{h}", [128, T], BF16, RET + 2048 * h) for h in range(QH)]
    sig = [at(f"sig{h}", [128, T], BF16, SB1 + 2048 * h) for h in range(QH)]
    b1s = [at(f"b1s{i}", [128, KC, 128], BF16, SB1 + 8192 * i) for i in range(2)]
    wqb = [at(f"wqb{i}", [128, KC, 128], BF16, WQ3 + 8192 * i) for i in range(2)]
    b3s = [at(f"b3s{i}", [128, KC, 128], BF16, WQ3 + 8192 * i) for i in range(2)]
    wos = [at(f"wos{i}", [128, QH, 4, 128], F32, XW + 16384 * i) for i in range(2)]
    wob = [at(f"wob{i}", [128, 1024], BF16, XW + 32768 + 2048 * i) for i in range(2)]
    xp = [at(f"xp{i}", [128, T], F32, XW + 36864 + 4096 * i) for i in range(2)]
    a_sb = [at(f"asb{i}", [128, 512], F32, XW + 45056 + 2048 * i) for i in range(4)]
    o_sb = [at(f"osb{i}", [128, 512], F32, XW + 53248 + 2048 * i) for i in range(4)]
    xs = [at(f"xs{i}", [128, T], F32, HW + 4096 * i) for i in range(2)]
    xb = [at(f"xbb{i}", [128, T], BF16, HW + 8192 + 2048 * i) for i in range(2)]
    xq = [at(f"xsq{i}", [128, T], BF16, HW + 12288 + 2048 * i) for i in range(2)]
    lt = [at(f"lt{i}", [128, T], F32, HW + 16384 + 4096 * i) for i in range(2)]
    r_m = at("r_m", [1, T], F32, HW + 24576)
    r_e2 = at("r_e2", [1, T], F32, HW + 28672)
    r_tp = at("r_tp", [1, T], F32, HW + 32768)
    r_rs = at("r_rs", [1, T], F32, HW + 36864)
    r_mr = at("r_mr", [1, T], F32, HW + 40960)
    rstd_b = at("rstd_b", [128, T], F32, HW + 45056)
    mr_b = at("mr_b", [128, T], F32, HW + 49152)
    s0c = at("s0c", [128, KC], F32, SMALL + 0)
    s2c = at("s2c", [128, KC], F32, SMALL + 128)
    s4c = at("s4c", [128, KC], F32, SMALL + 256)
    bic = at("bic", [128, KC], F32, SMALL + 384)
    bqc = at("bqc", [128, QH], F32, SMALL + 512)
    onesc = at("onesc", [128, 1], BF16, SMALL + 576)
    onesr = at("onesr", [1, 128], F32, SMALL + 640)
    mzf = at("mzf", [128, QH, 129], F32, SMALL + 1152)
    mzb = at("mzb", [128, QH, 129], BF16, SMALL + 5312)
    r_den = at("r_den", [1, T], F32, SMALL + 7392)
    denb = at("denb", [128, T], F32, SMALL + 11488)
    zer_c = at("zer_c", [128, 1], F32, SMALL + 23776)
    eps5 = at("eps5", [1, 1], F32, SMALL + 23808)
    eps6 = at("eps6", [1, 1], F32, SMALL + 23840)
    inv_c = at("inv_c", [1, 1], F32, SMALL + 23872)
    e_tmp = [at(f"e_tmp{i}", [128, 512], F32, SMALL + 15584 + 2048 * i) for i in range(2)]
    p_tmp = [at(f"p_tmp{i}", [128, 512], F32, SMALL + 19680 + 2048 * i) for i in range(2)]

    pb = [nc.alloc_psum_tensor(f"pb{i}", [128, 512], F32) for i in range(8)]

    p = Plan(nc)
    for s in ("s_sync", "s_scal", "s_gp", "s_out", "s_pe", "s_dve", "s_act", "s_pool"):
        p.sem(s)

    bank_free = [None] * 8

    wqT_v = wqT.rearrange("(k p) c -> p k c", p=128)
    b1T_v = b1T.rearrange("(k p) c -> p k c", p=128)
    b3T_v = b3T.rearrange("(k p) c -> p k c", p=128)
    woT_v = woT.rearrange("(h g p) c -> p h g c", g=4, p=128)
    mz_v = mz.rearrange("h d c -> d h c")

    p.op("sync", "dma_start", s0c[:], s0m, inc="s_sync", amt=16)
    p.op("sync", "dma_start", s2c[:], s2m, inc="s_sync", amt=16)
    p.op("sync", "dma_start", s4c[:], s4m, inc="s_sync", amt=16)
    p.op("sync", "dma_start", bic[:], bim, inc="s_sync", amt=16)
    p.op("sync", "dma_start", bqc[:], bqm, inc="s_sync", amt=16)
    t_mzf = p.op("sync", "dma_start", mzf[:], mz_v, inc="s_sync", amt=16)
    t_smalls = t_mzf
    p.op("vector", "memset", onesc[:], 1.0, inc="s_dve")
    p.op("vector", "memset", zer_c[:], 0.0, inc="s_dve")
    p.op("vector", "memset", eps5.bitcast(mybir.dt.uint32)[:], 925353388, inc="s_dve")
    p.op("vector", "memset", eps6.bitcast(mybir.dt.uint32)[:], 897988541, inc="s_dve")
    p.op("vector", "memset", inv_c.bitcast(mybir.dt.uint32)[:], 964689920, inc="s_dve")
    t_ones = p.op("vector", "memset", onesr[:], 1.0, inc="s_dve")
    t_mzb = p.op("vector", "tensor_copy", mzb[:], mzf[:], wait=[t_mzf], inc="s_dve")

    xs_free = [None, None]
    xb_free = [None, None]
    xq_free = [None, None]
    for k in range(KC):
        sl = k % 2
        t_x1 = p.op("sync", "dma_start", xs[sl][:], xT[k * 128:(k + 1) * 128, :],
                    wait=[xs_free[sl]], inc="s_sync", amt=16)
        t_cb = p.op("vector", "tensor_copy", xb[sl][:], xs[sl][:],
                    wait=[t_x1, xb_free[sl]], inc="s_dve")
        t_sq = p.op("scalar", "activation", xq[sl][:], xs[sl][:], AF.Square,
                    wait=[t_x1, xq_free[sl]], inc="s_act")
        xs_free[sl] = [t_cb, t_sq]
        w0 = [t_cb, t_sq] + ([t_ones] if k == 0 else [])
        p.op("tensor", "matmul", pb[0][0:1, :], onesc[:], xb[sl][:, 0:512],
             wait=w0, start=(k == 0), stop=(k == KC - 1))
        p.op("tensor", "matmul", pb[1][0:1, :], onesc[:], xb[sl][:, 512:1024],
             start=(k == 0), stop=(k == KC - 1))
        p.op("tensor", "matmul", pb[2][0:1, :], onesc[:], xq[sl][:, 0:512],
             start=(k == 0), stop=(k == KC - 1))
        tok_pe_a = p.op("tensor", "matmul", pb[3][0:1, :], onesc[:], xq[sl][:, 512:1024],
                        start=(k == 0), stop=(k == KC - 1), inc="s_pe")
        xb_free[sl] = tok_pe_a
        xq_free[sl] = tok_pe_a

    t1 = p.op("vector", "tensor_scalar", r_m[0:1, 0:512], pb[0][0:1, :], inv_c[:], None,
              ALU.mult, wait=[tok_pe_a], inc="s_dve")
    t2 = p.op("vector", "tensor_scalar", r_m[0:1, 512:1024], pb[1][0:1, :], inv_c[:], None,
              ALU.mult, inc="s_dve")
    t3 = p.op("vector", "tensor_scalar", r_e2[0:1, 0:512], pb[2][0:1, :], inv_c[:], None,
              ALU.mult, inc="s_dve")
    t4 = p.op("vector", "tensor_scalar", r_e2[0:1, 512:1024], pb[3][0:1, :], inv_c[:], None,
              ALU.mult, inc="s_dve")
    bank_free[0], bank_free[1], bank_free[2], bank_free[3] = t1, t2, t3, t4
    p.op("vector", "tensor_tensor", r_tp[:], r_m[:], r_m[:], ALU.mult, inc="s_dve")
    t_var = p.op("vector", "tensor_tensor", r_e2[:], r_e2[:], r_tp[:], ALU.subtract,
                 inc="s_dve")
    t_var = p.op("vector", "tensor_scalar", r_e2[:], r_e2[:], eps5[:], None,
                 ALU.add, inc="s_dve")
    p.op("scalar", "activation", r_tp[:], r_e2[:], AF.Ln,
         wait=[t_var], inc="s_act")
    t_rs = p.op("scalar", "activation", r_rs[:], r_tp[:], AF.Exp, scale=-0.5,
                inc="s_act")
    t_mrt = p.op("vector", "tensor_tensor", r_mr[:], r_m[:], r_rs[:], ALU.mult,
                 wait=[t_rs], inc="s_dve")
    p.op("tensor", "matmul", pb[4][:], onesr[:], r_rs[0:1, 0:512],
         wait=[t_rs, t_mrt], start=True, stop=True)
    p.op("tensor", "matmul", pb[5][:], onesr[:], r_rs[0:1, 512:1024],
         start=True, stop=True)
    p.op("tensor", "matmul", pb[6][:], onesr[:], r_mr[0:1, 0:512],
         start=True, stop=True)
    t_bc = p.op("tensor", "matmul", pb[7][:], onesr[:], r_mr[0:1, 512:1024],
                start=True, stop=True, inc="s_pe")
    p.op("vector", "tensor_copy", rstd_b[:, 0:512], pb[4][:], wait=[t_bc], inc="s_dve")
    p.op("vector", "tensor_copy", rstd_b[:, 512:1024], pb[5][:], inc="s_dve")
    p.op("vector", "tensor_copy", mr_b[:, 0:512], pb[6][:], inc="s_dve")
    t_bcd = p.op("vector", "tensor_copy", mr_b[:, 512:1024], pb[7][:], inc="s_dve")
    bank_free[4] = bank_free[5] = bank_free[6] = bank_free[7] = t_bcd

    lt_free = [None, None]
    t_xw = [None] * KC
    for k in range(KC):
        sl = k % 2
        t_x2 = p.op("sync", "dma_start", xs[sl][:], xT[k * 128:(k + 1) * 128, :],
                    wait=[xs_free[sl]], inc="s_sync", amt=16)
        t_mul = p.op("vector", "tensor_tensor", lt[sl][:], xs[sl][:], rstd_b[:],
                     ALU.mult, wait=[t_x2, t_bcd, lt_free[sl]], inc="s_dve")
        xs_free[sl] = t_mul
        eng, sem = ("vector", "s_dve") if k % 2 == 0 else ("gpsimd", "s_pool")
        t_xw[k] = p.op(eng, "tensor_tensor", xw[k][:], lt[sl][:], mr_b[:],
                       ALU.subtract, wait=[t_mul], inc=sem)
        lt_free[sl] = t_xw[k]
    t_xw_all = list(t_xw)

    wq_free = [None, None]
    et_free = [None, None]
    pt_free = [None, None]
    t_sig = [[None, None] for _ in range(QH)]
    t_c_end = None
    for qh in range(QH):
        sl = qh % 2
        t_wq = p.op("gpsimd", "dma_start", wqb[sl][:],
                    wqT_v[:, :, qh * 128:(qh + 1) * 128],
                    wait=[wq_free[sl]], inc="s_gp", amt=16)
        bA, bB = pb[sl * 2], pb[sl * 2 + 1]
        iA, iB = sl * 2, sl * 2 + 1
        w0 = [t_wq, t_smalls, bank_free[iA], bank_free[iB], t_xw_all]
        tq = None
        for k in range(KC):
            p.op("tensor", "matmul", bA[:], wqb[sl][:, k, :], xw[k][:, 0:512],
                 wait=(w0 if k == 0 else []), start=(k == 0), stop=(k == KC - 1))
            tq = p.op("tensor", "matmul", bB[:], wqb[sl][:, k, :], xw[k][:, 512:1024],
                      start=(k == 0), stop=(k == KC - 1),
                      inc=("s_pe" if k == KC - 1 else None))
        wq_free[sl] = tq
        t_c_end = tq
        for h, (bk, ib) in enumerate(((bA, iA), (bB, iB))):
            par = (qh * 2 + h) % 2
            te = p.op("vector", "tensor_scalar", e_tmp[par][:], bk[:],
                      bqc[:, qh:qh + 1], zer_c[:], ALU.add, ALU.min,
                      wait=[tq, et_free[par]], inc="s_dve")
            tx = p.op("scalar", "activation", e_tmp[par][:], e_tmp[par][:], AF.Exp,
                      wait=[te], inc="s_act")
            tp2 = p.op("vector", "tensor_scalar", p_tmp[par][:], bk[:],
                       bqc[:, qh:qh + 1], zer_c[:], ALU.add, ALU.max,
                       wait=[pt_free[par]], inc="s_dve")
            bank_free[ib] = tp2
            ts = p.op("gpsimd", "tensor_tensor", sig[qh][:, h * 512:(h + 1) * 512],
                      e_tmp[par][:], p_tmp[par][:], ALU.add,
                      wait=[tx, tp2], inc="s_pool")
            t_sig[qh][h] = ts
            et_free[par] = ts
            pt_free[par] = ts

    t_xws = [None] * KC
    for k in range(KC):
        t_xws[k] = p.op("vector", "tensor_scalar", xw[k][:], xw[k][:],
                        s0c[:, k:k + 1], None, ALU.mult,
                        wait=([t_c_end] if k == 0 else []), inc="s_dve")
    t_xws_all = list(t_xws)

    rden_free = None
    denb_free = None
    t_ret = [None] * QH
    t_d_pe = None
    for h in range(QH):
        par = h % 2
        bn0, bn1, bd0, bd1 = (pb[par * 4], pb[par * 4 + 1],
                              pb[par * 4 + 2], pb[par * 4 + 3])
        i0, i1, i2, i3 = par * 4, par * 4 + 1, par * 4 + 2, par * 4 + 3
        w0 = [t_sig[h][0], t_sig[h][1], t_mzb,
              bank_free[i0], bank_free[i1], bank_free[i2], bank_free[i3]]
        p.op("tensor", "matmul", bn0[:], mzb[:, h, 0:128], sig[h][:, 0:512],
             wait=w0, start=True, stop=True)
        p.op("tensor", "matmul", bn1[:], mzb[:, h, 0:128], sig[h][:, 512:1024],
             start=True, stop=True)
        p.op("tensor", "matmul", bd0[0:1, :], mzb[:, h, 128:129], sig[h][:, 0:512],
             start=True, stop=True)
        t_n = p.op("tensor", "matmul", bd1[0:1, :], mzb[:, h, 128:129],
                   sig[h][:, 512:1024], start=True, stop=True, inc="s_pe")
        t_d_pe = t_n
        p.op("vector", "tensor_scalar", r_den[0:1, 0:512], bd0[0:1, :],
             eps6[:], None, ALU.add, wait=[t_n, rden_free], inc="s_dve")
        p.op("vector", "tensor_scalar", r_den[0:1, 512:1024], bd1[0:1, :],
             eps6[:], None, ALU.add, inc="s_dve")
        t_rc = p.op("vector", "reciprocal", r_den[:], r_den[:], inc="s_dve")
        p.op("tensor", "matmul", bd0[:], onesr[:], r_den[0:1, 0:512],
             wait=[t_rc], start=True, stop=True)
        t_db = p.op("tensor", "matmul", bd1[:], onesr[:], r_den[0:1, 512:1024],
                    start=True, stop=True, inc="s_pe")
        t_d_pe = t_db
        tc0 = p.op("vector", "tensor_copy", denb[:, 0:512], bd0[:],
                   wait=[t_db, denb_free], inc="s_dve")
        tc1 = p.op("vector", "tensor_copy", denb[:, 512:1024], bd1[:], inc="s_dve")
        bank_free[i2], bank_free[i3] = tc0, tc1
        tr0 = p.op("vector", "tensor_tensor", ret[h][:, 0:512], bn0[:],
                   denb[:, 0:512], ALU.mult, wait=[tc1], inc="s_dve")
        tr1 = p.op("vector", "tensor_tensor", ret[h][:, 512:1024], bn1[:],
                   denb[:, 512:1024], ALU.mult, inc="s_dve")
        bank_free[i0], bank_free[i1] = tr0, tr1
        rden_free = t_db
        denb_free = tr1
        t_ret[h] = tr1
    t_ret_all = list(t_ret)

    b1_free = [t_d_pe, t_d_pe]
    t_hw = [None] * KC
    t_f_end = None
    for mh in range(KC):
        sl = mh % 2
        t_b1 = p.op("gpsimd", "dma_start", b1s[sl][:],
                    b1T_v[:, :, mh * 128:(mh + 1) * 128],
                    wait=[b1_free[sl]], inc="s_gp", amt=16)
        t_cv = p.op("vector", "tensor_scalar",
                    b1s[sl][:].rearrange("p k c -> p (k c)"),
                    b1s[sl][:].rearrange("p k c -> p (k c)"),
                    2.0, -1.0, ALU.mult, ALU.add, wait=[t_b1], inc="s_dve")
        bA, bB = pb[sl * 2], pb[sl * 2 + 1]
        iA, iB = sl * 2, sl * 2 + 1
        w0 = [t_cv, bank_free[iA], bank_free[iB], t_xws_all]
        th = None
        for k in range(KC):
            p.op("tensor", "matmul", bA[:], b1s[sl][:, k, :], xw[k][:, 0:512],
                 wait=(w0 if k == 0 else []), start=(k == 0), stop=(k == KC - 1))
            th = p.op("tensor", "matmul", bB[:], b1s[sl][:, k, :], xw[k][:, 512:1024],
                      start=(k == 0), stop=(k == KC - 1),
                      inc=("s_pe" if k == KC - 1 else None))
        b1_free[sl] = th
        t_f_end = th
        ta = p.op("scalar", "activation", hw[mh][:, 0:512], bA[:], AF.Identity,
                  scale=s2c[:, mh:mh + 1], wait=[th], inc="s_act")
        tb = p.op("scalar", "activation", hw[mh][:, 512:1024], bB[:], AF.Identity,
                  scale=s2c[:, mh:mh + 1], inc="s_act")
        bank_free[iA], bank_free[iB] = ta, tb
        t_hw[mh] = tb
    t_hw_all = list(t_hw)

    wos_free = [t_f_end, t_f_end]
    wob_free = [t_f_end, t_f_end]
    b3_free = [wq_free[0], wq_free[1]]
    xp_free = [t_f_end, t_f_end]
    asb_free = [t_f_end] * 4
    osb_free = [t_f_end] * 4
    t_wos = [None] * KC
    n_out = 0

    t_wos[0] = p.op("sync", "dma_start", wos[0][:], woT_v[:, :, :, 0:128],
                    wait=[wos_free[0]], inc="s_sync", amt=16)
    for c in range(KC):
        sl = c % 2
        bb = sl * 4
        pa0, pa1, pf0, pf1 = pb[bb], pb[bb + 1], pb[bb + 2], pb[bb + 3]
        ia0, ia1, if0, if1 = bb, bb + 1, bb + 2, bb + 3
        if c + 1 < KC:
            t_wos[c + 1] = p.op("sync", "dma_start", wos[(c + 1) % 2][:],
                                woT_v[:, :, :, (c + 1) * 128:(c + 2) * 128],
                                wait=[wos_free[(c + 1) % 2]], inc="s_sync", amt=16)
        t_b3 = p.op("gpsimd", "dma_start", b3s[sl][:],
                    b3T_v[:, :, c * 128:(c + 1) * 128],
                    wait=[b3_free[sl]], inc="s_gp", amt=16)
        t_b3c = p.op("vector", "tensor_scalar",
                     b3s[sl][:].rearrange("p k c -> p (k c)"),
                     b3s[sl][:].rearrange("p k c -> p (k c)"),
                     2.0, -1.0, ALU.mult, ALU.add, wait=[t_b3], inc="s_dve")
        t_xp = p.op("scalar", "dma_start", xp[sl][:], xT[c * 128:(c + 1) * 128, :],
                    wait=[xp_free[sl]], inc="s_scal", amt=16)
        wsl = wos[sl][:]
        wob3 = wob[sl][:].rearrange("p (h c) -> p h c", c=128)
        p.op("gpsimd", "tensor_tensor", wob3, wsl[:, :, 0, :], wsl[:, :, 1, :],
             ALU.add, wait=[t_wos[c], wob_free[sl]], inc="s_pool")
        p.op("gpsimd", "tensor_tensor", wob3, wob3, wsl[:, :, 2, :],
             ALU.add, inc="s_pool")
        t_wob = p.op("gpsimd", "tensor_tensor", wob3, wob3, wsl[:, :, 3, :],
                     ALU.add, inc="s_pool")
        wos_free[sl] = t_wob
        w0 = [t_wob, bank_free[ia0], bank_free[ia1], t_ret_all]
        t_at = None
        for h in range(QH):
            p.op("tensor", "matmul", pa0[:], wob[sl][:, h * 128:(h + 1) * 128],
                 ret[h][:, 0:512], wait=(w0 if h == 0 else []),
                 start=(h == 0), stop=(h == QH - 1))
            t_at = p.op("tensor", "matmul", pa1[:], wob[sl][:, h * 128:(h + 1) * 128],
                        ret[h][:, 512:1024], start=(h == 0), stop=(h == QH - 1),
                        inc=("s_pe" if h == QH - 1 else None))
        wob_free[sl] = t_at
        w0 = [t_b3c, bank_free[if0], bank_free[if1], t_hw_all]
        t_ff = None
        for k in range(KC):
            p.op("tensor", "matmul", pf0[:], b3s[sl][:, k, :], hw[k][:, 0:512],
                 wait=(w0 if k == 0 else []), start=(k == 0), stop=(k == KC - 1))
            t_ff = p.op("tensor", "matmul", pf1[:], b3s[sl][:, k, :],
                        hw[k][:, 512:1024], start=(k == 0), stop=(k == KC - 1),
                        inc=("s_pe" if k == KC - 1 else None))
        b3_free[sl] = t_ff
        t_ax1 = None
        for h, (pa, pf, ipa, ipf) in enumerate(((pa0, pf0, ia0, if0),
                                                (pa1, pf1, ia1, if1))):
            slot = sl * 2 + h
            ta = p.op("scalar", "activation", a_sb[slot][:], pa[:], AF.Identity,
                      bias=bic[:, c:c + 1], wait=[t_at, asb_free[slot]], inc="s_act")
            bank_free[ipa] = ta
            tax = p.op("gpsimd", "tensor_tensor", a_sb[slot][:], a_sb[slot][:],
                       xp[sl][:, h * 512:(h + 1) * 512], ALU.add,
                       wait=[ta, t_xp], inc="s_pool")
            t_ax1 = tax
            to = p.op("vector", "scalar_tensor_tensor", o_sb[slot][:], pf[:],
                      s4c[:, c:c + 1], a_sb[slot][:], ALU.mult, ALU.add,
                      wait=[t_ff, tax, osb_free[slot]], inc="s_dve")
            bank_free[ipf] = to
            asb_free[slot] = to
            tst = p.op("sync", "dma_start",
                       outT[c * 128:(c + 1) * 128, h * 512:(h + 1) * 512],
                       o_sb[slot][:], wait=[to], inc="s_out", amt=16)
            osb_free[slot] = tst
            n_out += 1
        xp_free[sl] = t_ax1

    p.op("sync", None, wait=[("s_out", 16 * n_out)])

    p.emit()
    return nc


_NC = None


def _get_nc():
    global _NC
    if _NC is None:
        _NC = build_nc()
    return _NC


def kernel(**inputs):
    g = lambda n: np.asarray(inputs[n])
    x = g("L0_attn_hidden_states")[0]
    Wq = g("L0_attn_Wq"); bq = g("L0_attn_bq")
    Wo = g("L0_attn_Wo"); M = g("L0_attn_M"); z = g("L0_attn_z")
    s0 = g("L0_ffn_scaling0"); s2 = g("L0_ffn_scaling2"); s4 = g("L0_ffn_scaling4")
    bias = g("L0_ffn_bias")
    w1b = g("L0_ffn_w1_bits"); w3b = g("L0_ffn_w3_bits")

    C = np.ascontiguousarray
    xT = C(x.T)
    shared = {
        "wqT": C(Wq.reshape(32, 128, D)[::4].reshape(1024, D).T),
        "bqm": C(bq.reshape(32, 128)[::4].T),
        "mz": C(np.concatenate([M, z[:, :, None]], axis=2)),
        "b1T": C(w1b.T), "b3T": C(w3b.T), "woT": C(Wo.T),
        "s0m": C(s0.reshape(KC, 128).T), "s2m": C(s2.reshape(KC, 128).T),
        "s4m": C(s4.reshape(KC, 128).T), "bim": C(bias.reshape(KC, 128).T),
    }
    in_maps = [dict(shared, xT=C(xT[:, c * T:(c + 1) * T])) for c in range(NCORES)]
    nc = _get_nc()
    res = run_bass_kernel_spmd(nc, in_maps, core_ids=list(range(NCORES)))
    out = np.concatenate([np.asarray(r["outT"], dtype=np.float32).T
                          for r in res.results], axis=0)
    return out.reshape(1, 8192, 4096)
